# version 02 (final)
# baseline (speedup 1.0000x reference)
import math
from contextlib import ExitStack

import numpy as np
import ml_dtypes

import concourse.bass as bass
import concourse.mybir as mybir
from concourse.bass_utils import run_bass_kernel_spmd

F32 = mybir.dt.float32
BF16 = mybir.dt.bfloat16
F16 = mybir.dt.float16
I32 = mybir.dt.int32
AF = mybir.ActivationFunctionType
ALU = mybir.AluOpType

B = 4
S = 8192
HID = 2048
NH = 32
HD = 64
BS = 256
IFF = 5632
NCORES = 8
TCORE = S // 2
TPASS = 1024
THALF = 2048
LN_EPS = 1e-5
ATTN_EPS = 1e-5
FFN_EPS = 1e-6
NEG_BIG = -1e30
JH = 256
JI = 704

DBG_EXPOSE = set()
DBG_SKIP = set()
DBG_PARANOID = True

_factor = 1.0 + 1e-05
SLOPES = [(1.0 / 2.0 ** (8.0 / NH)) ** (h + 1) * _factor for h in range(NH)]
BDECAY = [math.exp(-s * BS) for s in SLOPES]


def _perm_h():
    r = np.arange(HID)
    c = r // 1024
    m = (r // 128) % 8
    p = r % 128
    return (8 * (128 * c + p) + m).astype(np.int64)


def _perm_i():
    out = np.empty(IFF, np.int64)
    idx = 0
    for k in range(5):
        for m in range(8):
            for p in range(128):
                out[idx] = 8 * (128 * k + p) + m
                idx += 1
    for mp in range(4):
        for half in range(2):
            m = 2 * mp + half
            for j in range(64):
                out[idx] = 8 * (128 * 5 + j) + m
                idx += 1
    assert idx == IFF
    return out


PERM_H = _perm_h()
PERM_I = _perm_i()


def _q_rows():
    ch = np.arange(HID)
    return (192 * (ch // 64) + (ch % 64)).astype(np.int64)


def _kv_rows():
    ch = np.arange(HID)
    k = 192 * (ch // 64) + 64 + (ch % 64)
    v = 192 * (ch // 64) + 128 + (ch % 64)
    return np.concatenate([k, v]).astype(np.int64)


GU_WIN = [(k * 1024, 1024) for k in range(5)] + [(5120, 512)]


def _gu_sets():
    sets = []
    for wi, (lo, w) in enumerate(GU_WIN):
        for part in range(2):
            for q in range(w // 256):
                sets.append((wi, part, q, part * IFF + lo + 256 * q))
    return sets


GU_SETS = _gu_sets()
GU_SET_IDX = {(wi, part, q): i for i, (wi, part, q, _) in enumerate(GU_SETS)}
GU_SET_ELEMS = 128 * 16 * 256
GU_STREAM_ELEMS = len(GU_SETS) * GU_SET_ELEMS

DN_SLAB_COLS = 44 * 128
DN_SLAB_ELEMS = 128 * DN_SLAB_COLS
DN_STREAM_ELEMS = 16 * DN_SLAB_ELEMS


def _qa_decode(qa):
    part = qa // 22
    qq = qa % 22
    wi = min(qq // 4, 5)
    q = qq - 4 * wi
    return wi, part, q


class Ctr:
    __slots__ = ("name", "sem", "n")

    def __init__(self, name):
        self.name = name
        self.sem = None
        self.n = 0


class Plan:
    ENGINES = ("sync", "gpsimd", "vector", "scalar", "tensor")
    SERIAL = ("vector", "scalar", "gpsimd")

    DMA_POOL = {"sync": 10, "gpsimd": 4}

    def __init__(self):
        self.steps = {e: [] for e in self.ENGINES}
        self.ctr = {e: Ctr("c_" + e) for e in self.ENGINES}
        self.dpool = {q: [Ctr(f"d_{q}{i}") for i in range(n)]
                      for q, n in self.DMA_POOL.items()}
        self.drr = {q: 0 for q in self.DMA_POOL}
        self._lastwait = {e: {} for e in self.ENGINES}

    def step(self, eng, emit, waits=(), inc=0, dma=False):
        ctr = self.ctr[eng]
        lw = self._lastwait[eng]
        real = []
        serial = eng in self.SERIAL
        if serial and not dma:
            real.append((ctr, ctr.n))
            if inc == 0:
                inc = 1
        for t in waits:
            if t is None:
                continue
            c, v, is_dma = t
            if c is ctr and not is_dma and not serial:
                continue
            if lw.get(c.name, -1) >= v:
                continue
            if c is ctr and v <= real[0][1] if (serial and not dma and real and real[0][0] is ctr) else False:
                continue
            lw[c.name] = v
            real.append((c, v))
        if inc:
            ctr.n += inc
        tok = (ctr, ctr.n, dma)
        self.steps[eng].append((real, emit, ctr if inc else None, inc))
        return tok

    def dma(self, eng, out, in_, waits=(), **kw):
        pool = self.dpool[eng]
        sem = pool[self.drr[eng] % len(pool)]
        self.drr[eng] += 1
        gate = (sem, sem.n, True)
        sem.n += 16
        tok = (sem, sem.n, True)
        lw = self._lastwait[eng]
        real = []
        for t in list(waits) + [gate]:
            if t is None:
                continue
            c, v, is_dma = t
            if c is self.ctr[eng] and not is_dma and eng not in self.SERIAL:
                continue
            if lw.get(c.name, -1) >= v:
                continue
            lw[c.name] = v
            real.append((c, v))
        self.steps[eng].append(
            (real, lambda e: e.dma_start(out=out, in_=in_, **kw), sem, 16))
        return tok

    def dma_toks(self, eng):
        return [(s, s.n, True) for s in self.dpool[eng]]

    def tok(self, eng, dma=False):
        c = self.ctr[eng]
        return (c, c.n, dma)

    def barrier(self, engines=None, nop=None):
        engines = engines or self.ENGINES
        toks = [self.tok(e) for e in engines]
        if len(engines) == len(self.ENGINES):
            toks += self.dma_toks("sync") + self.dma_toks("gpsimd")
        out = {}
        for e in engines:
            out[e] = self.step(e, lambda en: None, waits=toks)
        return [self.tok(e) for e in engines]

    def replay(self, eng_name, eng):
        for waits, emit, ctr, inc in self.steps[eng_name]:
            for (c, v) in waits:
                eng.wait_ge(c.sem, v)
            instr = emit(eng)
            if ctr is not None:
                if instr is None:
                    instr = eng.nop()
                instr.then_inc(ctr.sem, inc)


def evac_copy(eng):
    if eng == "scalar":
        return lambda out, in_: (lambda e: e.copy(out=out, in_=in_))
    return lambda out, in_: (lambda e: e.tensor_copy(out=out, in_=in_))


class MK:
    def __init__(self, nc, es, stop_after=None):
        self.nc = nc
        self.es = es
        self.npass = TCORE // TPASS
        self.nblk = TCORE // BS
        self.stop_after = stop_after
        self.pl = Plan()
        self.pwar = {}
        self._decl()

    def _decl(self):
        nc, es = self.nc, self.es
        T = TCORE
        self.x_in = nc.dram_tensor("x_in", [HID, T], F32, kind="ExternalInput").ap()
        self.w_q = nc.dram_tensor("w_q", [16, 128, 2048], F32, kind="ExternalInput").ap()
        self.w_kv = nc.dram_tensor("w_kv", [HID, 2 * HID], F32, kind="ExternalInput").ap()
        self.w_k = nc.dram_tensor("w_k", [16, 128, 2048], F32, kind="ExternalInput").ap()
        self.w_gate = nc.dram_tensor("w_gate", [16, 128, 2048], F32, kind="ExternalInput").ap()
        self.w_out = nc.dram_tensor("w_out", [16, 128, 2048], F32, kind="ExternalInput").ap()
        self.nw_in = nc.dram_tensor("nw_in", [HID], F32, kind="ExternalInput").ap()
        self.sgate_in = nc.dram_tensor("sgate_in", [128, 1], F32, kind="ExternalInput").ap()
        self.gup_in = nc.dram_tensor("gup_in", [JH, 2 * IFF], I32, kind="ExternalInput").ap()
        self.gus_in = nc.dram_tensor("gus_in", [JH, 2 * IFF], F16, kind="ExternalInput").ap()
        self.dnp_in = nc.dram_tensor("dnp_in", [JI, HID], I32, kind="ExternalInput").ap()
        self.dns_in = nc.dram_tensor("dns_in", [JI, HID], F16, kind="ExternalInput").ap()
        self.y_out = nc.dram_tensor("y_out", [HID, T], F32, kind="ExternalOutput").ap()

        def sc(name, shape, dtype):
            kind = "ExternalOutput" if name in DBG_EXPOSE else "Internal"
            return nc.dram_tensor(name, shape, dtype, kind=kind).ap()

        self.xn_s = sc("xn_s", [128, 16 * T], BF16)
        self.q_s = sc("q_s", [128, 16 * T], BF16)
        self.kf_s = sc("kf_s", [128, 16 * T], BF16)
        self.kt_s = sc("kt_s", [T, HID], BF16)
        self.vt_s = sc("vt_s", [T, HID], BF16)
        self.km_s = sc("km_s", [self.nblk, 128, 1024], BF16)
        self.s_s = sc("s_s", [128, 16 * T], F32)
        self.gu_s = sc("gu_s", [GU_STREAM_ELEMS], BF16)
        self.dn_s = sc("dn_s", [DN_STREAM_ELEMS], BF16)
        self.cc_in = sc("cc_in", [128, 1024], F32)
        self.dbg = (nc.dram_tensor("dbg", [128, 8192], F32, kind="ExternalOutput").ap()
                    if "dbg" in DBG_EXPOSE else None)
        self.cc_out = nc.dram_tensor("cc_out", [256, 1024], F32, kind="Internal").ap()

        def sb(name, shape, dtype):
            return es.enter_context(nc.sbuf_tensor(name, shape, dtype))

        self.nw = sb("nw", [128, 16], F32)
        self.sgate = sb("sgate", [128, 1], F32)
        self.ones_col = sb("ones_col", [128, 1], BF16)
        self.ones_row = sb("ones_row", [1, 128], F32)
        self.qd = sb("qd", [128, 16 * BS], BF16)
        self.kd = sb("kd", [128, 2 * HID], BF16)
        self.bd = sb("bd", [128, 1024], BF16)
        self.iota_r = sb("iota_r", [128, BS], F32)
        self.iota_c = sb("iota_c", [128, 2], F32)
        self.scl = sb("scl", [128, 16], F32)
        self.S = sb("S", [128, 1024], F32)
        self.S_bf = sb("S_bf", [128, 1024], BF16)
        self.mt = sb("mt", [128, 2 * BS], F32)
        self.str_ = sb("str_", [1, 4096], F32)

        self.cpool = sb("cpool", [1, 64], F32)
        self.AR_BYTES = 163 * 1024
        self.arena = sb("arena", [128, self.AR_BYTES // 2], BF16)
        self.cvals = {}
        cv = [LN_EPS, FFN_EPS, -0.5, 1.0 / HID] + [float(s) for s in SLOPES]
        for i, v in enumerate(dict.fromkeys(cv)):
            self.cvals[v] = i
            nc.const_aps.register(F32, v, self.cpool[0:1, i:i + 1])

        def ps(name):
            return es.enter_context(nc.psum_tensor(name, [128, 512], F32))

        self.p_aw = ps("p_aw")
        self.p_x = ps("p_x")
        self.p_at = ps("p_at")
        self.p_bc = ps("p_bc")
        self.p_mm = [ps(f"p_mm{i}") for i in range(4)]

        for e in Plan.ENGINES:
            self.pl.ctr[e].sem = es.enter_context(nc.semaphore("sem_" + e))
        for q, pool in self.pl.dpool.items():
            for c in pool:
                c.sem = es.enter_context(nc.semaphore("sem_" + c.name))

    def av(self, off_nelem_dtype_or_off, nelem=None, dtype=None):
        if nelem is None:
            off, nelem, dtype = off_nelem_dtype_or_off
        else:
            off = off_nelem_dtype_or_off
        esz = {BF16: 2, F32: 4, I32: 4, F16: 2}[dtype]
        assert off % esz == 0
        assert off + nelem * esz <= self.AR_BYTES, (off, nelem, dtype)
        a = self.arena[:].bitcast(dtype)
        return a[:, off // esz: off // esz + nelem]

    def fence(self, eng):
        pl = self.pl
        cell = self.str_[0:1, 4088:4096]
        if eng == "vector":
            return pl.step(eng, lambda e: e.tensor_copy(
                out=cell.bitcast(F32) if False else cell, in_=cell), inc=1)
        return pl.step(eng, lambda e: e.copy(out=cell, in_=cell), inc=1)

    def pw(self, bank):
        t = self.pwar.get(id(bank))
        if t is None:
            return []
        return list(t) if isinstance(t, list) else [t]

    def set_pw(self, bank, tok):
        self.pwar[id(bank)] = tok


def emit_constants(mk):
    pl = mk.pl

    def gpi(out, pattern, base, chm):
        return lambda e: e.iota(out=out, pattern=pattern, base=base,
                                channel_multiplier=chm,
                                allow_small_or_imprecise_dtypes=True)

    t_cp = None
    for v, i in mk.cvals.items():
        t_cp = pl.step("gpsimd", (lambda v, i: lambda e: e.memset(
            mk.cpool[0:1, i:i + 1], v))(v, i), inc=1)
    mk.t_cpool = t_cp
    pl.step("gpsimd", lambda e: e.memset(mk.ones_col[:], 1.0), inc=1)
    pl.step("gpsimd", lambda e: e.memset(mk.ones_row[:], 1.0), inc=1)
    t_ir = pl.step("gpsimd", gpi(mk.iota_r[:], [[1, BS]], 1, 0), inc=1)
    t_ic = pl.step("gpsimd", gpi(mk.iota_c[:], [[128, 2]], -255, 1), inc=1)
    t_m0 = pl.step("gpsimd", gpi(mk.mt[:, 0:BS], [[-1, BS]], 0, 1), inc=1)
    t_m1 = pl.step("gpsimd", gpi(mk.mt[:, BS:2 * BS], [[-1, BS]], 128, 1), inc=1)
    tmp = mk.av(0, 512, F32)
    pl.step("vector", lambda e: e.tensor_scalar(
        out=tmp, in0=mk.mt[:], scalar1=0.0, scalar2=NEG_BIG,
        op0=ALU.is_gt, op1=ALU.mult), waits=[t_m0, t_m1], inc=1)
    mk.t_mt = pl.step("vector", lambda e: e.tensor_add(mk.mt[:], mk.mt[:], tmp),
                      inc=1)

    for z in range(16):
        t_nw = pl.dma("sync", mk.nw[:, z:z + 1],
                      mk.nw_in[128 * z:128 * (z + 1)].rearrange("(p o) -> p o", o=1))
    mk.t_nw = t_nw
    mk.t_sg = pl.dma("sync", mk.sgate[:], mk.sgate_in)

    sc_t = None
    for z in range(16):
        for half in range(2):
            sc_t = pl.step("gpsimd", (lambda z, half: lambda e: e.memset(
                mk.scl[64 * half:64 * (half + 1), z:z + 1],
                -SLOPES[2 * z + half]))(z, half), inc=1)
    t_qd = None
    for z in range(16):
        t_qd = pl.step("scalar", (lambda z: lambda e: e.activation(
            out=mk.qd[:, z * BS:(z + 1) * BS], in_=mk.iota_r[:], func=AF.Exp,
            scale=mk.scl[:, z:z + 1]))(z), waits=[t_ir, sc_t], inc=1)
    mk.t_qd = t_qd

    slt = mk.av(2048, 2048, F32)
    sl_t = None
    for h in range(NH):
        sl_t = pl.step("gpsimd", (lambda h: lambda e: e.memset(
            slt[:, 64 * h:64 * (h + 1)], SLOPES[h]))(h), inc=1)
    t_kd = None
    for r in range(2):
        t_kd = pl.step("scalar", (lambda r: lambda e: e.activation(
            out=mk.kd[:, r * HID:(r + 1) * HID], in_=slt, func=AF.Exp,
            scale=mk.iota_c[:, r:r + 1]))(r), waits=[sl_t, t_ic], inc=1)
    mk.t_kd = t_kd

    t_bd = None
    for hp in range(16):
        for half in range(2):
            t_bd = pl.step("gpsimd", (lambda hp, half: lambda e: e.memset(
                mk.bd[64 * half:64 * (half + 1), 64 * hp:64 * (hp + 1)],
                BDECAY[2 * hp + half]))(hp, half), inc=1)
    mk.t_bd = t_bd
    mk.t_const = [mk.t_mt, mk.t_kd, mk.t_qd, mk.t_bd, mk.t_nw, mk.t_sg, mk.t_cpool]


def _dq_lay(base, ncols):
    o = base
    lay = {}
    lay["pk"] = (o, ncols, I32); o += 4 * ncols
    lay["sch"] = (o, ncols, F16); o += 2 * ncols
    lay["sc"] = (o, ncols, F32); o += 4 * ncols
    for i in range(6):
        lay[f"t{i}"] = (o, ncols, I32); o += 4 * ncols
    lay["out"] = (o, 8 * ncols, BF16); o += 16 * ncols
    return lay, o


def emit_dequant(mk):
    pl = mk.pl
    lay_v, end_v = _dq_lay(0, 1024)
    lay_g, end_g = _dq_lay(end_v, 1024)
    assert end_g <= mk.AR_BYTES, end_g

    state = {
        "vector": {"war": list(mk.t_const), "lay": lay_v, "dmaq": "sync"},
        "gpsimd": {"war": list(mk.t_const), "lay": lay_g, "dmaq": "gpsimd"},
    }

    def chain(eng, lay, rows, ncols, k, use_stt):
        pk = mk.av(lay["pk"])[0:rows, 0:ncols]
        t = [mk.av(lay[f"t{i}"])[0:rows, 0:ncols] for i in range(6)]
        n, m, u, b, sbt, v = t
        pl.step(eng, lambda e: e.tensor_scalar(
            out=n, in0=pk, scalar1=4 * k, scalar2=15,
            op0=ALU.logical_shift_right, op1=ALU.bitwise_and))
        pl.step(eng, lambda e: e.tensor_scalar(
            out=m, in0=n, scalar1=7, scalar2=None, op0=ALU.bitwise_and))
        pl.step(eng, lambda e: e.tensor_scalar(
            out=u, in0=m, scalar1=0x40, scalar2=0x3F00, op0=ALU.mult, op1=ALU.add))
        if use_stt:
            pl.step(eng, lambda e: e.scalar_tensor_tensor(
                out=b, in0=m, scalar=0x3F00, in1=u, op0=ALU.mult, op1=ALU.min))
        else:
            pl.step(eng, lambda e: e.tensor_scalar(
                out=b, in0=m, scalar1=0x3F00, scalar2=None, op0=ALU.mult))
            pl.step(eng, lambda e: e.tensor_tensor(out=b, in0=b, in1=u, op=ALU.min))
        pl.step(eng, lambda e: e.tensor_scalar(
            out=sbt, in0=n, scalar1=8, scalar2=-2147483648.0,
            op0=ALU.is_ge, op1=ALU.mult))
        if use_stt:
            return pl.step(eng, lambda e: e.scalar_tensor_tensor(
                out=v, in0=b, scalar=65536, in1=sbt, op0=ALU.mult, op1=ALU.add),
                inc=1)
        pl.step(eng, lambda e: e.tensor_scalar(
            out=v, in0=b, scalar1=65536, scalar2=None, op0=ALU.mult))
        return pl.step(eng, lambda e: e.tensor_tensor(
            out=v, in0=v, in1=sbt, op=ALU.add), inc=1)

    def do_tile(eng, pk_src, sc_src, rows, ncols, writer):
        st = state[eng]
        lay, dmaq = st["lay"], st["dmaq"]
        pk = mk.av(lay["pk"])[0:rows, 0:ncols]
        sch = mk.av(lay["sch"])[0:rows, 0:ncols]
        sc = mk.av(lay["sc"])[0:rows, 0:ncols]
        dq_o = mk.av(lay["out"])
        t_pk = pl.dma(dmaq, pk, pk_src, waits=st["war"])
        t_sc = pl.dma(dmaq, sch, sc_src)
        pl.step(eng, lambda e: e.tensor_copy(out=sc, in_=sch),
                waits=[t_pk, t_sc], inc=1)
        use_stt = (eng == "vector")
        last = []
        for k in range(8):
            chain(eng, lay, rows, ncols, k, use_stt)
            v = mk.av(lay["t5"])[0:rows, 0:ncols]
            plane = dq_o[0:rows, k * ncols:(k + 1) * ncols]
            t_pl = pl.step(eng, (lambda plane, v, sc: lambda e: e.tensor_tensor(
                out=plane, in0=v.bitcast(F32), in1=sc, op=ALU.mult))(plane, v, sc),
                inc=1)
            last = writer(k, plane, t_pl, dmaq)
        st["war"] = last

    def gu_writer(kh, c0):
        def w(k, plane, t_pl, dmaq):
            toks = []
            kc = 8 * kh + k
            for qa in range(c0 // 256, c0 // 256 + 4):
                wi, part, q = _qa_decode(qa)
                si = GU_SET_IDX[(wi, part, q)]
                base = si * GU_SET_ELEMS + kc * 256
                dst = mk.gu_s[base:base + 128 * 4096 - (4096 - 256)] \
                    .rearrange("(p j) -> p j", p=1) if False else None
                dst = mk.gu_s[base: base + 127 * 4096 + 256] \
                    .rearrange("(p j) -> p j", p=128) if False else None
                dst = _strided_dram(mk.gu_s, base, 128, 4096, 256)
                src = plane[:, (qa * 256 - c0):(qa * 256 - c0) + 256]
                toks.append(pl.dma(dmaq, dst, src, waits=[t_pl]))
            return toks
        return w

    def dn_writer(kp, rows, cc2):
        def w(k, plane, t_pl, dmaq):
            toks = []
            for zl in range(8):
                z = 8 * cc2 + zl
                if kp < 5:
                    base = z * DN_SLAB_ELEMS + (8 * kp + k) * 128
                else:
                    base = (z * DN_SLAB_ELEMS + (40 + k // 2) * 128
                            + (k % 2) * 64 * DN_SLAB_COLS)
                dst = _strided_dram(mk.dn_s, base, rows, DN_SLAB_COLS, 128)
                src = plane[0:rows, zl * 128:(zl + 1) * 128]
                toks.append(pl.dma(dmaq, dst, src, waits=[t_pl]))
            return toks
        return w

    jobs = []
    for kh in range(2):
        for cc in range(11):
            jobs.append(("gu", kh, cc))
    for kp in range(6):
        for cc2 in range(2):
            jobs.append(("dn", kp, cc2))

    for idx, job in enumerate(jobs):
        eng = "vector"
        if job[0] == "gu":
            _, kh, cc = job
            c0 = cc * 1024
            do_tile(eng, mk.gup_in[128 * kh:128 * (kh + 1), c0:c0 + 1024],
                    mk.gus_in[128 * kh:128 * (kh + 1), c0:c0 + 1024],
                    128, 1024, gu_writer(kh, c0))
        else:
            _, kp, cc2 = job
            rows = 128 if kp < 5 else 64
            do_tile(eng, mk.dnp_in[128 * kp:128 * kp + rows, 1024 * cc2:1024 * (cc2 + 1)],
                    mk.dns_in[128 * kp:128 * kp + rows, 1024 * cc2:1024 * (cc2 + 1)],
                    rows, 1024, dn_writer(kp, rows, cc2))

    mk.t_deq_sy = pl.dma_toks("sync")
    mk.t_deq_gp = pl.dma_toks("gpsimd")
    mk.t_deq_ve = pl.tok("vector")
    mk.t_deq_arena = mk.t_deq_sy + mk.t_deq_gp + [mk.t_deq_ve]


def _strided_dram(flat, base, rows, row_stride, ncols):
    span = (rows - 1) * row_stride + ncols
    v = flat[base: base + span]
    return bass.AP(v.tensor, v.offset, [[row_stride, rows], [1, ncols]])


def p1_lay(mk):
    lay = {}
    o = 0
    lay["xn"] = (o, 16 * THALF, BF16); o += 32 * THALF
    lay["wf"] = [(o, 2048, F32), (o + 8192, 2048, F32)]; o += 16384
    lay["wb"] = [(o, 8192, BF16), (o + 16384, 8192, BF16)]; o += 32768
    lay["stg"] = [(o, 512, BF16), (o + 1024, 512, BF16)]; o += 2048
    lay["ktb"] = (o, 4096, BF16); o += 8192
    lay["vtb"] = (o, 4096, BF16); o += 8192
    lay["kpr"] = (o, 4096, BF16); o += 8192
    lay["kms"] = (o, 1024, BF16); o += 2048
    assert o <= mk.AR_BYTES, o
    lay["bcast"] = lay["wf"][0]
    lay["sq"] = (lay["wb"][0][0], 1024, BF16)
    return lay


def emit_phase1(mk):
    pl = mk.pl
    lay = p1_lay(mk)
    TH = THALF
    xn_all = mk.av(lay["xn"])
    bcast = mk.av(lay["bcast"])
    sq = mk.av(lay["sq"])
    stg = [mk.av(lay["stg"][i]) for i in range(2)]
    wf = [mk.av(lay["wf"][i]) for i in range(2)]
    wb = [mk.av(lay["wb"][i]) for i in range(2)]
    str_ = mk.str_
    arena_war = list(mk.t_deq_arena)
    s_war = [None, None]

    for hf in range(2):
        th0 = hf * TH

        xn_done = []
        for st in range(2):
            t0 = th0 + st * TPASS
            t_sq_mm = None
            for z in range(16):
                dst = xn_all[:, z * TH + st * TPASS: z * TH + (st + 1) * TPASS]
                t_x = pl.dma("gpsimd", dst,
                             mk.x_in[128 * z:128 * (z + 1), t0:t0 + TPASS],
                             waits=arena_war)
                for nh in range(2):
                    pl.step("tensor", (lambda z, nh, dst: lambda e: e.matmul(
                        out=mk.p_mm[nh][0:1, :], lhsT=mk.ones_col[:],
                        rhs=dst[:, 512 * nh:512 * (nh + 1)],
                        start=(z == 0), stop=(z == 15),
                        skip_group_check=True))(z, nh, dst),
                        waits=[t_x] + (mk.pw(mk.p_mm[nh]) if z == 0 else []),
                        inc=(1 if z == 15 else 0))
                t_sq = pl.step("scalar", (lambda dst: lambda e: e.activation(
                    out=sq, in_=dst, func=AF.Square))(dst),
                    waits=[t_x] + ([t_sq_mm] if t_sq_mm else []) +
                    (arena_war if z == 0 else []), inc=1)
                for nh in range(2):
                    t_sq_mm = pl.step("tensor", (lambda z, nh: lambda e: e.matmul(
                        out=mk.p_mm[2 + nh][0:1, :], lhsT=mk.ones_col[:],
                        rhs=sq[:, 512 * nh:512 * (nh + 1)],
                        start=(z == 0), stop=(z == 15),
                        skip_group_check=True))(z, nh),
                        waits=[t_sq] + (mk.pw(mk.p_mm[2 + nh]) if z == 0 else []),
                        inc=(1 if nh == 1 else 0))
            t_mm = pl.tok("tensor")
            bc_done = []
            for nh in range(2):
                so_ = 2048 * nh
                mu = str_[0:1, so_ + 0:so_ + 512]
                rstd = str_[0:1, so_ + 512:so_ + 1024]
                mrs = str_[0:1, so_ + 1024:so_ + 1536]
                scr = str_[0:1, so_ + 1536:so_ + 2048]
                t_mu = pl.step("scalar", (lambda nh, mu: lambda e: e.mul(
                    out=mu, in_=mk.p_mm[nh][0:1, :], mul=1.0 / HID))(nh, mu),
                    waits=[t_mm], inc=1)
                t_mu2 = pl.step("scalar", (lambda mu, scr: lambda e: e.activation(
                    out=scr, in_=mu, func=AF.Square))(mu, scr),
                    waits=[t_mu], inc=1)
                t_var = pl.step("vector", (lambda nh, scr: lambda e: e.scalar_tensor_tensor(
                    out=scr, in0=mk.p_mm[2 + nh][0:1, :], scalar=1.0 / HID, in1=scr,
                    op0=ALU.mult, op1=ALU.subtract))(nh, scr),
                    waits=[t_mu2, t_mm], inc=1)
                t_ln = pl.step("scalar", (lambda scr, rstd: lambda e: e.activation(
                    out=rstd, in_=scr, func=AF.Ln, bias=LN_EPS))(scr, rstd),
                    waits=[t_var], inc=1)
                t_rs = pl.step("scalar", (lambda rstd: lambda e: e.activation(
                    out=rstd, in_=rstd, func=AF.Exp, scale=-0.5))(rstd),
                    waits=[t_ln], inc=1)
                t_mrs = pl.step("vector", (lambda mu, rstd, mrs: lambda e: e.tensor_mul(
                    mrs, mu, rstd))(mu, rstd, mrs), waits=[t_rs], inc=1)
                mk.set_pw(mk.p_mm[nh], t_mrs)
                mk.set_pw(mk.p_mm[2 + nh], t_var)
                for i, src in enumerate((rstd, mrs)):
                    t_b = pl.step("tensor", (lambda src: lambda e: e.matmul(
                        out=mk.p_bc[:, :], lhsT=mk.ones_row[:], rhs=src,
                        start=True, stop=True))(src),
                        waits=[t_mrs] + mk.pw(mk.p_bc), inc=1)
                    t_c = pl.step("vector", (lambda i, nh: lambda e: e.tensor_copy(
                        out=bcast[:, 1024 * i + 512 * nh:1024 * i + 512 * (nh + 1)],
                        in_=mk.p_bc[:, :]))(i, nh),
                        waits=[t_b] + arena_war, inc=1)
                    mk.set_pw(mk.p_bc, t_c)
                    bc_done.append(t_c)
            for z in range(16):
                dst = xn_all[:, z * TH + st * TPASS: z * TH + (st + 1) * TPASS]
                pl.step("vector", (lambda dst: lambda e: e.tensor_mul(
                    dst, dst, bcast[:, 0:1024]))(dst), waits=bc_done, inc=1)
                t2 = pl.step("vector", (lambda dst: lambda e: e.tensor_sub(
                    dst, dst, bcast[:, 1024:2048]))(dst), inc=1)
                xn_done.append(t2)
                pl.dma("sync",
                       mk.xn_s[:, (t0 // TPASS) * 16384 + z * 1024:
                               (t0 // TPASS) * 16384 + (z + 1) * 1024],
                       dst, waits=[t2])
        t_xn = xn_done[-1]
        mk.t_ln_pe = pl.tok("tensor")
        if mk.dbg is not None and hf == 0:
            pl.dma("sync", mk.dbg[:, 0:2048], bcast, waits=[t_xn])
            pl.dma("sync", mk.dbg[0:1, 2048:2048 + 4096], mk.str_[:], waits=[t_xn])
        if mk.stop_after == "ln":
            return

        w_war = [t_xn, mk.t_ln_pe]
        for wsrc, dsts in ((mk.w_q, mk.q_s), (mk.w_k, mk.kf_s)):
            for z in range(16):
                sl = z % 2
                t_wf = pl.dma("sync", wf[sl][:, 0:2048], wsrc[z],
                              waits=[w_war[sl], t_xn, mk.t_ln_pe])
                t_wb = pl.step("scalar", (lambda sl: lambda e: e.copy(
                    out=wb[sl][:, 0:2048], in_=wf[sl][:, 0:2048]))(sl),
                    waits=[t_wf, mk.t_ln_pe if z < 2 else None], inc=1)
                for nh in range(4):
                    pb = mk.p_mm[nh % 4]
                    t_m = None
                    for kc in range(16):
                        t_m = pl.step("tensor", (lambda sl, kc, nh, pb: lambda e: e.matmul(
                            out=pb[:, :],
                            lhsT=wb[sl][:, 128 * kc:128 * (kc + 1)],
                            rhs=xn_all[:, kc * TH + nh * 512: kc * TH + (nh + 1) * 512],
                            start=(kc == 0), stop=(kc == 15)))(sl, kc, nh, pb),
                            waits=([t_wb, t_xn] + mk.pw(pb)) if kc == 0 else [],
                            inc=(1 if kc == 15 else 0))
                    ssl = nh % 2
                    t_e = pl.step("scalar", (lambda pb, ssl: lambda e: e.activation(
                        out=stg[ssl], in_=pb[:, :], func=AF.Silu))(pb, ssl),
                        waits=[t_m] + ([s_war[ssl]] if s_war[ssl] else []), inc=1)
                    mk.set_pw(pb, t_e)
                    w_war[sl] = t_e
                    gb = (th0 + nh * 512) // BS
                    dst = _q_dst(mk, z, gb, dsts)
                    t_d = pl.dma("sync", dst, stg[ssl], waits=[t_e])
                    s_war[ssl] = t_d

        if mk.stop_after == "qproj":
            return
        wb_war = [w_war[0], w_war[1]]
        for piece in range(8):
            psl = piece % 2
            stack = wb[psl]
            t_cast = None
            for kc in range(16):
                fsl = kc % 2
                t_wf = pl.dma("sync", wf[fsl][:, 0:512],
                              mk.w_kv[128 * kc:128 * (kc + 1),
                                      512 * piece:512 * (piece + 1)],
                              waits=[w_war[fsl]] + ([wb_war[psl]] if kc < 2 else []))
                t_cast = pl.step("scalar", (lambda fsl, kc, stack: lambda e: e.copy(
                    out=stack[:, 512 * kc:512 * (kc + 1)],
                    in_=wf[fsl][:, 0:512]))(fsl, kc, stack),
                    waits=[t_wf], inc=1)
                w_war[fsl] = t_cast
            dst_s = mk.kt_s if piece < 4 else mk.vt_s
            c0 = 512 * (piece % 4)
            t_e = None
            for tch in range(TH // 128):
                pb = mk.p_mm[tch % 4]
                t_m = None
                for kc in range(16):
                    t_m = pl.step("tensor", (lambda kc, tch, pb, stack: lambda e: e.matmul(
                        out=pb[:, :],
                        lhsT=xn_all[:, kc * TH + 128 * tch:kc * TH + 128 * (tch + 1)],
                        rhs=stack[:, 512 * kc:512 * (kc + 1)],
                        start=(kc == 0), stop=(kc == 15)))(kc, tch, pb, stack),
                        waits=([t_cast, t_xn] + mk.pw(pb)) if kc == 0 else [],
                        inc=(1 if kc == 15 else 0))
                ssl = tch % 2
                t_e = pl.step("scalar", (lambda pb, ssl: lambda e: e.activation(
                    out=stg[ssl], in_=pb[:, :], func=AF.Silu))(pb, ssl),
                    waits=[t_m] + ([s_war[ssl]] if s_war[ssl] else []), inc=1)
                mk.set_pw(pb, t_e)
                t_d = pl.dma("sync",
                             dst_s[th0 + 128 * tch:th0 + 128 * (tch + 1), c0:c0 + 512],
                             stg[ssl], waits=[t_e])
                s_war[ssl] = t_d
            wb_war[psl] = t_e
        arena_war = [pl.tok("tensor")] + pl.dma_toks("sync")
    if mk.stop_after == "kvproj":
        return
    t_kv_sy = pl.dma_toks("sync")

    ktb = mk.av(lay["ktb"])
    vtb = mk.av(lay["vtb"])
    kpr = mk.av(lay["kpr"])
    kms = mk.av(lay["kms"])
    t_s0 = pl.step("vector", lambda e: e.memset(mk.S[:], 0.0), inc=1)
    war_kt = t_kv_sy + arena_war
    war_km = []
    t_up2 = t_s0
    t_kp_pe = None
    for blk in range(mk.nblk):
        tb = blk * BS
        t_kt = pl.dma("sync", ktb.rearrange("p (c n) -> p c n", c=2),
                      mk.kt_s[tb:tb + 256, :].rearrange("(c p) n -> p c n", p=128),
                      waits=war_kt)
        t_vt = pl.dma("sync", vtb.rearrange("p (c n) -> p c n", c=2),
                      mk.vt_s[tb:tb + 256, :].rearrange("(c p) n -> p c n", p=128))
        if "kpr" in DBG_SKIP:
            t_kp = pl.step("vector", lambda e: e.tensor_copy(out=kpr, in_=ktb),
                           waits=[t_kt, mk.t_kd] + ([t_kp_pe] if t_kp_pe else []), inc=1)
        else:
            t_kp = pl.step("vector", lambda e: e.tensor_mul(kpr, ktb, mk.kd[:]),
                           waits=[t_kt, mk.t_kd] + ([t_kp_pe] if t_kp_pe else []), inc=1)
        t_mm_last = {0: None, 1: None}
        t_prevg = None
        for hp in range(16):
            bank = mk.p_aw if hp < 8 else mk.p_x
            for half in range(2):
                h = 2 * hp + half
                for c in range(2):
                    hard = [(t_prevg[0], t_prevg[1], True)] \
                        if (c == 0 and t_prevg is not None) else []
                    t_mmk = pl.step("tensor",
                        (lambda bank, hp, half, h, c: lambda e: e.matmul(
                            out=bank[64 * half:64 * (half + 1),
                                     (hp % 8) * 64:(hp % 8) * 64 + 64],
                            lhsT=kpr[:, 2048 * c + 64 * h:2048 * c + 64 * h + 64],
                            rhs=vtb[:, 2048 * c + 64 * h:2048 * c + 64 * h + 64],
                            start=(c == 0), stop=(c == 1),
                            tile_position=(0, 64 * half),
                            skip_group_check=True))(bank, hp, half, h, c),
                        waits=([t_kp, t_vt] + hard + (mk.pw(bank)
                               if hp % 8 == 0 and half == 0 else []))
                        if c == 0 else [],
                        inc=(1 if c == 1 else 0))
                t_prevg = t_mmk
                t_mm_last[hp // 8] = t_mmk
        t_kp_pe = t_mm_last[1]
        ev = []
        for i, bank in enumerate((mk.p_aw, mk.p_x)):
            t_ev = pl.step("scalar", (lambda i, bank: lambda e: e.copy(
                out=kms[:, 512 * i:512 * (i + 1)], in_=bank[:, :]))(i, bank),
                waits=[t_mm_last[i]] + war_km, inc=1)
            ev.append(t_ev)
        if "kmdma" in DBG_SKIP:
            t_kmd = ev[-1]
        else:
            t_kmd = pl.dma("sync", mk.km_s[blk], kms, waits=ev)
        war_km = [t_kmd]
        if "sup" in DBG_SKIP:
            t_up1 = pl.step("vector", lambda e: e.tensor_mul(mk.S[:], mk.S[:], mk.bd[:]),
                            waits=[t_up2, mk.t_bd, t_mm_last[0], t_mm_last[1], ev[0], ev[1]], inc=1)
            t_up2a = t_up1
            t_up2 = t_up1
        else:
            t_up1 = pl.step("vector", lambda e: e.tensor_mul(mk.S[:], mk.S[:], mk.bd[:]),
                            waits=[t_up2, mk.t_bd], inc=1)
            t_up2a = pl.step("vector", lambda e: e.tensor_add(
                mk.S[:, 0:512], mk.S[:, 0:512], mk.p_aw[:, :]),
                waits=[t_up1, t_mm_last[0], ev[0]], inc=1)
            t_up2 = pl.step("vector", lambda e: e.tensor_add(
                mk.S[:, 512:1024], mk.S[:, 512:1024], mk.p_x[:, :]),
                waits=[t_mm_last[1], ev[1]], inc=1)
        mk.set_pw(mk.p_aw, [t_up2a, ev[0]])
        mk.set_pw(mk.p_x, [t_up2, ev[1]])
        war_kt = [t_kp]
        if mk.stop_after == "km" + str(blk + 1):
            break
    mk.t_scan = t_up2
    pl.dma("sync", mk.cc_in, mk.S[:], waits=[mk.t_scan])
    mk.t_km_sy = pl.dma_toks("sync")
    mk.t_p1_pe = pl.tok("tensor")


def _q_dst(mk, z, gb, tens):
    flat = tens
    base = gb * 4096 + z * 256
    return bass.AP(flat.tensor, flat.offset + base,
                   [list(flat.ap[0]), [4096, 2], [1, 256]])


def emit_collective(mk):
    pl = mk.pl
    lay = p1_lay(mk)
    t_ci = pl.dma_toks("sync")
    t_cc = pl.step("gpsimd", lambda e: e.collective_compute(
        "AllGather", ALU.bypass,
        replica_groups=[[0, 1], [2, 3], [4, 5], [6, 7]],
        ins=[mk.cc_in], outs=[mk.cc_out]), waits=t_ci, inc=1)
    stg = mk.av(lay["ktb"][0], 1024, F32)
    t_rd = pl.dma("sync", stg, mk.cc_out[0:128, :],
                  waits=[t_cc, mk.t_scan, mk.t_p1_pe] + mk.t_km_sy)
    mk.t_s_init = pl.step("vector", lambda e: e.tensor_scalar(
        out=mk.S[:], in0=stg, scalar1=mk.sgate[:, 0:1], scalar2=None,
        op0=ALU.mult), waits=[t_rd, mk.t_sg, mk.t_scan], inc=1)


def p2_lay(mk):
    T = TPASS
    lay = {}
    o = 0
    lay["xnp"] = (o, 16 * T, BF16); o += 32 * T
    lay["attn"] = (o, 16 * T, BF16); o += 32 * T
    lay["emask"] = (o, NH * 512, BF16); o += NH * 1024
    lay["qblk"] = (o, 4096, BF16); o += 8192
    lay["kfb"] = (o, 4096, BF16); o += 8192
    lay["qp"] = (o, 4096, BF16); o += 8192
    lay["vtb"] = (o, 4096, BF16); o += 8192
    lay["kmb"] = (o, 1024, BF16); o += 2048
    lay["aw"] = [(o, 512, BF16), (o + 1024, 512, BF16)]; o += 2048
    lay["wf"] = [(o, 2048, F32), (o + 8192, 2048, F32)]; o += 16384
    lay["wb"] = (o, 2048, BF16); o += 4096
    lay["gstg"] = (o, 512, BF16); o += 1024
    lay["sstg"] = [(o, 512, F32), (o + 2048, 512, F32)]; o += 4096
    lay["rstd"] = (o, 1024, F32); o += 4096
    lay["sq"] = (o, 1024, BF16); o += 2048
    assert o <= mk.AR_BYTES, o
    lay["f_y"] = (0, 16 * T, BF16)
    lay["f_hact"] = (32 * 1024, 44 * T, BF16)
    lay["f_guw"] = [(120 * 1024, 4096, BF16),
                    (128 * 1024, 4096, BF16)]
    lay["f_dnw"] = [(120 * 1024, 44 * 128, BF16),
                    (131 * 1024 + 256, 44 * 128, BF16)]
    lay["f_gbuf"] = (143 * 1024, 1024, BF16)
    lay["f_srd"] = [(145 * 1024, 512, F32), (147 * 1024, 512, F32)]
    lay["f_xrd"] = [(149 * 1024, 512, F32), (151 * 1024, 512, F32)]
    lay["f_ost"] = [(153 * 1024, 512, F32), (155 * 1024, 512, F32)]
    return lay


def emit_pass(mk, p):
    pl = mk.pl
    lay = p2_lay(mk)
    T = TPASS
    t0 = p * T
    war = mk.pass_war if hasattr(mk, "pass_war") else \
        [mk.t_s_init, mk.t_p1_pe, pl.tok("scalar")] + mk.t_km_sy + \
        pl.dma_toks("sync")

    pl.barrier()
    xnp = mk.av(lay["xnp"])
    attn = mk.av(lay["attn"])
    emask = mk.av(lay["emask"])
    qblk = mk.av(lay["qblk"])
    kfb = mk.av(lay["kfb"])
    qp = mk.av(lay["qp"])
    vtb = mk.av(lay["vtb"])
    kmb = mk.av(lay["kmb"])
    awb = [mk.av(lay["aw"][i]) for i in range(2)]
    sq = mk.av(lay["sq"])
    rstd_b = mk.av(lay["rstd"])
    str_ = mk.str_

    t_xnp = pl.dma("sync", xnp, mk.xn_s[:, 16 * t0:16 * t0 + 16 * T], waits=war)
    t_em = None
    for h in range(NH):
        t_em = pl.step("scalar", (lambda h: lambda e: e.activation(
            out=emask[:, h * 512:(h + 1) * 512], in_=mk.mt[:], func=AF.Exp,
            scale=SLOPES[h]))(h), waits=[mk.t_mt] + (war if h == 0 else []), inc=1)

    aw_war = [None, None]
    t_sup = mk.t_s_init if p == 0 else mk.t_sup
    t_blk_pe = None
    t_kmb_prev = None
    for b in range(4):
        gb = 4 * p + b
        tb = t0 + b * BS
        bw = war + ([t_blk_pe] if t_blk_pe else [])
        t_qb = pl.dma("sync", qblk, mk.q_s[:, gb * 4096:(gb + 1) * 4096], waits=bw)
        t_kmb = pl.dma("sync", kmb, mk.km_s[gb],
                       waits=bw + ([t_kmb_prev] if t_kmb_prev else []))
        t_vt = pl.dma("sync", vtb.rearrange("p (c n) -> p c n", c=2),
                      mk.vt_s[tb:tb + 256, :].rearrange("(c p) n -> p c n", p=128),
                      waits=bw)
        t_kf = pl.dma("sync", kfb, mk.kf_s[:, gb * 4096:(gb + 1) * 4096], waits=bw)
        t_qp = pl.step("vector", lambda e: e.tensor_mul(qp, qblk, mk.qd[:]),
                       waits=[t_qb, mk.t_qd] + ([t_blk_pe] if t_blk_pe else []),
                       inc=1)
        t_sbf = pl.step("vector", lambda e: e.tensor_copy(out=mk.S_bf[:], in_=mk.S[:]),
                        waits=[t_sup] + ([t_blk_pe] if t_blk_pe else []), inc=1)
        for hp in range(16):
            at_bank = (mk.p_at, mk.p_bc)[hp % 2]
            t_at = {0: None, 1: None}
            for half in range(2):
                h = 2 * hp + half
                sc_bank = (mk.p_aw, mk.p_x)[h % 2]
                pb = 64 * half
                t_sc = None
                for jc in range(2):
                    t_sc = pl.step("tensor",
                        (lambda sc_bank, jc, hp, pb: lambda e: e.matmul(
                            out=sc_bank[:, 256 * jc:256 * (jc + 1)],
                            lhsT=kfb[pb:pb + 64,
                                     256 * hp + 128 * jc:256 * hp + 128 * (jc + 1)],
                            rhs=qblk[pb:pb + 64, 256 * hp:256 * (hp + 1)],
                            start=True, stop=True,
                            skip_group_check=True))(sc_bank, jc, hp, pb),
                        waits=([t_kf, t_qb] + mk.pw(sc_bank)) if jc == 0 else [],
                        inc=(1 if jc == 1 else 0))
                t_awm = pl.step("vector", (lambda h, sc_bank: lambda e: e.tensor_mul(
                    awb[h % 2], sc_bank[:, :],
                    emask[:, 512 * h:512 * (h + 1)]))(h, sc_bank),
                    waits=[t_sc, t_em] + ([aw_war[h % 2]] if aw_war[h % 2] else []),
                    inc=1)
                mk.set_pw(sc_bank, t_awm)
                hard = [] if half == 0 else \
                    [(t_at[0][0], t_at[0][1], True)]
                pl.step("tensor", (lambda at_bank, h, half: lambda e: e.matmul(
                    out=at_bank[64 * half:64 * half + 64, 0:256],
                    lhsT=vtb[:, 64 * h:64 * h + 64],
                    rhs=awb[h % 2][:, 0:256],
                    start=True, stop=False,
                    tile_position=(0, 64 * half),
                    skip_group_check=True))(at_bank, h, half),
                    waits=[t_awm, t_vt] + hard + (mk.pw(at_bank) if half == 0 else []))
                pl.step("tensor", (lambda at_bank, h, half: lambda e: e.matmul(
                    out=at_bank[64 * half:64 * half + 64, 0:256],
                    lhsT=vtb[:, 2048 + 64 * h:2048 + 64 * h + 64],
                    rhs=awb[h % 2][:, 256:512],
                    start=False, stop=False,
                    tile_position=(0, 64 * half),
                    skip_group_check=True))(at_bank, h, half))
                t_at[half] = pl.step("tensor",
                    (lambda at_bank, half, hp, pb: lambda e: e.matmul(
                        out=at_bank[64 * half:64 * half + 64, 0:256],
                        lhsT=mk.S_bf[pb:pb + 64, 64 * hp:64 * hp + 64],
                        rhs=qp[pb:pb + 64, 256 * hp:256 * (hp + 1)],
                        start=False, stop=True,
                        tile_position=(64 * half, 64 * half),
                        skip_group_check=True))(at_bank, half, hp, pb),
                    waits=[t_qp, t_sbf], inc=1)
                aw_war[h % 2] = t_at[half]
            eng = "scalar" if hp % 2 else "vector"
            cp = evac_copy(eng)
            t_ev = pl.step(eng, cp(attn[:, hp * T + b * BS: hp * T + (b + 1) * BS],
                                   at_bank[:, 0:256]),
                           waits=[t_at[0], t_at[1]], inc=1)
            mk.set_pw(at_bank, t_ev)
            if DBG_PARANOID:
                pl.barrier(engines=("vector", "scalar", "tensor"))
        t_u1 = pl.step("vector", lambda e: e.tensor_mul(mk.S[:], mk.S[:], mk.bd[:]),
                       waits=[t_sbf], inc=1)
        t_sup = pl.step("vector", lambda e: e.tensor_add(mk.S[:], mk.S[:], kmb),
                        waits=[t_u1, t_kmb], inc=1)
        t_kmb_prev = t_sup
        pl.barrier()
        t_blk_pe = pl.tok("tensor")
    mk.t_sup = t_sup
    pl.barrier()
    t_attn = [pl.tok("vector"), pl.tok("scalar")]
    if mk.stop_after == "p2blocks":
        pl.dma("gpsimd", mk.s_s[:, 0:16 * T], attn, waits=t_attn + [pl.tok("tensor")])
        return

    t_sq_mm = None
    for z in range(16):
        t_sq = pl.step("scalar", (lambda z: lambda e: e.activation(
            out=sq, in_=attn[:, z * T:(z + 1) * T], func=AF.Square))(z),
            waits=t_attn + ([t_sq_mm] if t_sq_mm else []) + (war if z == 0 else []),
            inc=1)
        for nh in range(2):
            t_sq_mm = pl.step("tensor", (lambda z, nh: lambda e: e.matmul(
                out=mk.p_mm[nh][0:1, :], lhsT=mk.ones_col[:],
                rhs=sq[:, 512 * nh:512 * (nh + 1)],
                start=(z == 0), stop=(z == 15),
                skip_group_check=True))(z, nh),
                waits=[t_sq] + (mk.pw(mk.p_mm[nh]) if z == 0 else []),
                inc=(1 if nh == 1 else 0))
        if DBG_PARANOID:
            pl.barrier(engines=("scalar", "tensor"))
    t_mm = pl.tok("tensor")

    def rstd_rows(t_mm, banks, eps, extra_war=()):
        done = []
        for nh in range(2):
            r0 = str_[0:1, 2048 * nh:2048 * nh + 512]
            t_l = pl.step("scalar", (lambda nh, r0: lambda e: e.activation(
                out=r0, in_=banks[nh][0:1, :], func=AF.Ln, scale=1.0 / HID,
                bias=eps))(nh, r0), waits=[t_mm], inc=1)
            t_r = pl.step("scalar", (lambda r0: lambda e: e.activation(
                out=r0, in_=r0, func=AF.Exp, scale=-0.5))(r0),
                waits=[t_l], inc=1)
            mk.set_pw(banks[nh], t_r)
            t_b = pl.step("tensor", (lambda r0: lambda e: e.matmul(
                out=mk.p_bc[:, :], lhsT=mk.ones_row[:], rhs=r0,
                start=True, stop=True))(r0), waits=[t_r] + mk.pw(mk.p_bc), inc=1)
            t_c = pl.step("vector", (lambda nh: lambda e: e.tensor_copy(
                out=rstd_b[:, 512 * nh:512 * (nh + 1)], in_=mk.p_bc[:, :]))(nh),
                waits=[t_b] + list(extra_war), inc=1)
            mk.set_pw(mk.p_bc, t_c)
            done.append(t_c)
        return done

    pl.barrier()
    t_rb = rstd_rows(t_mm, (mk.p_mm[0], mk.p_mm[1]), ATTN_EPS, war)
    t_an = {}
    for z in range(16):
        t_an[z] = pl.step("vector", (lambda z: lambda e: e.scalar_tensor_tensor(
            out=attn[:, z * T:(z + 1) * T], in0=attn[:, z * T:(z + 1) * T],
            scalar=mk.nw[:, z:z + 1], in1=rstd_b, op0=ALU.mult,
            op1=ALU.mult))(z), waits=t_rb + [mk.t_nw], inc=1)

    wf = [mk.av(lay["wf"][i]) for i in range(2)]
    wbg = mk.av(lay["wb"])
    gstg = mk.av(lay["gstg"])
    w_war = [None, None]
    t_gst = None
    t_mprev = None
    t_ag_prev = None
    for z in range(16):
        sl = z % 2
        t_wf = pl.dma("sync", wf[sl][:, 0:2048], mk.w_gate[z],
                      waits=([w_war[sl]] if w_war[sl] else war))
        t_wb = pl.step("scalar", (lambda sl: lambda e: e.copy(
            out=wbg, in_=wf[sl][:, 0:2048]))(sl),
            waits=[t_wf] + ([t_mprev] if t_mprev else []) +
            ([t_gst] if t_gst else []), inc=1)
        for nh in range(2):
            pb = mk.p_mm[2 + nh]
            t_m = None
            for kc in range(16):
                t_m = pl.step("tensor", (lambda kc, nh, pb: lambda e: e.matmul(
                    out=pb[:, :], lhsT=wbg[:, 128 * kc:128 * (kc + 1)],
                    rhs=xnp[:, kc * T + nh * 512: kc * T + (nh + 1) * 512],
                    start=(kc == 0), stop=(kc == 15)))(kc, nh, pb),
                    waits=([t_wb, t_xnp] + mk.pw(pb)) if kc == 0 else [],
                    inc=(1 if kc == 15 else 0))
            t_g = pl.step("scalar", (lambda pb: lambda e: e.activation(
                out=gstg, in_=pb[:, :], func=AF.Sigmoid))(pb),
                waits=[t_m] + ([t_gst] if t_gst else [])
                + ([t_ag_prev] if t_ag_prev else []), inc=1)
            t_gst = t_g
            t_mprev = t_m
            mk.set_pw(pb, t_g)
            t_ag = pl.step("vector", (lambda z, nh: lambda e: e.tensor_mul(
                attn[:, z * T + nh * 512: z * T + (nh + 1) * 512],
                attn[:, z * T + nh * 512: z * T + (nh + 1) * 512], gstg))(z, nh),
                waits=[t_g, t_an[z]], inc=1)
            t_ag_prev = t_ag
        w_war[sl] = t_ag
        if DBG_PARANOID:
            pl.barrier()
    mk.fence("vector")
    t_ag_all = pl.tok("vector")

    if mk.stop_after == "p2an":
        pl.dma("gpsimd", mk.s_s[:, 0:16 * T], attn,
               waits=[pl.tok("vector"), pl.tok("scalar"), pl.tok("tensor")])
        return
    pl.barrier()
    sstg = [mk.av(lay["sstg"][i]) for i in range(2)]
    s_war = [None, None]
    t_e = None
    for z in range(16):
        sl = z % 2
        t_wf = pl.dma("sync", wf[sl][:, 0:2048], mk.w_out[z],
                      waits=[w_war[sl]] if w_war[sl] else [])
        t_wb = pl.step("scalar", (lambda sl: lambda e: e.copy(
            out=wbg, in_=wf[sl][:, 0:2048]))(sl),
            waits=[t_wf, t_gst] + ([t_mprev] if t_mprev else []), inc=1)
        t_wb = mk.fence("scalar")
        for nh in range(2):
            pb = mk.p_mm[nh]
            t_m = None
            for kc in range(16):
                t_m = pl.step("tensor", (lambda kc, nh, pb: lambda e: e.matmul(
                    out=pb[:, :], lhsT=wbg[:, 128 * kc:128 * (kc + 1)],
                    rhs=attn[:, kc * T + nh * 512: kc * T + (nh + 1) * 512],
                    start=(kc == 0), stop=(kc == 15)))(kc, nh, pb),
                    waits=([t_wb, t_ag_all] + mk.pw(pb)) if kc == 0 else [],
                    inc=(1 if kc == 15 else 0))
            ssl = nh
            t_e = pl.step("vector", (lambda pb, ssl: lambda e: e.tensor_copy(
                out=sstg[ssl], in_=pb[:, :]))(pb, ssl),
                waits=[t_m] + ([s_war[ssl]] if s_war[ssl] else []), inc=1)
            mk.set_pw(pb, t_e)
            t_sd = pl.dma("sync",
                          mk.s_s[:, z * 4096 + t0 + 512 * nh:
                                 z * 4096 + t0 + 512 * (nh + 1)],
                          sstg[ssl], waits=[t_e])
            s_war[ssl] = t_sd
            t_mprev = t_m
        w_war[sl] = t_e
        if DBG_PARANOID:
            pl.barrier()
    t_s_sy = pl.dma_toks("sync")
    t_op_pe = pl.tok("tensor")
    if mk.stop_after == "p2s":
        pl.dma("gpsimd",
               mk.y_out[0:512, :].rearrange("(p a) t -> p (a t)", p=128), attn,
               waits=[pl.tok("vector"), pl.tok("scalar"), pl.tok("tensor")])
        return

    pl.barrier()
    hact = mk.av(lay["f_hact"])
    guw = [mk.av(lay["f_guw"][i]) for i in range(2)]
    ffn_war = [t_e, t_op_pe, pl.tok("scalar")] + t_s_sy
    gw_war = [None, None]
    if not hasattr(mk, "t_hact"):
        mk.t_hact = {}
    t_hact = {}
    set_i = 0
    for wi in range(6):
        nq = GU_WIN[wi][1] // 256
        for part in range(2):
            for q in range(nq):
                sl = set_i % 2
                si = GU_SET_IDX[(wi, part, q)]
                src = mk.gu_s[si * GU_SET_ELEMS:(si + 1) * GU_SET_ELEMS] \
                    .rearrange("(p n) -> p n", p=128)
                t_w = pl.dma("sync", guw[sl], src,
                             waits=([gw_war[sl]] if gw_war[sl] else ffn_war)
                             + (mk.t_deq_gp if p == 0 and set_i < 2 else []))
                for mt2 in range(2):
                    tile = (8 * wi + 2 * q + mt2) if wi < 5 else (40 + 2 * q + mt2)
                    for nh in range(2):
                        pb = mk.p_mm[2 * mt2 + nh]
                        t_m = None
                        for kc in range(16):
                            t_m = pl.step("tensor",
                                (lambda sl, kc, mt2, nh, pb: lambda e: e.matmul(
                                    out=pb[:, :],
                                    lhsT=guw[sl][:, 256 * kc + 128 * mt2:
                                                 256 * kc + 128 * (mt2 + 1)],
                                    rhs=xnp[:, kc * T + nh * 512:
                                            kc * T + (nh + 1) * 512],
                                    start=(kc == 0), stop=(kc == 15)))(sl, kc, mt2, nh, pb),
                                waits=([t_w, t_xnp] + mk.pw(pb)) if kc == 0 else [],
                                inc=(1 if kc == 15 else 0))
                        dst = hact[:, tile * T + nh * 512: tile * T + (nh + 1) * 512]
                        if part == 0:
                            hw = [mk.t_hact.get((tile, nh))] if p > 0 else []
                            t_h = pl.step("scalar",
                                (lambda pb, dst: lambda e: e.activation(
                                    out=dst, in_=pb[:, :], func=AF.Silu))(pb, dst),
                                waits=[t_m] + hw + (ffn_war if set_i == 0 else []),
                                inc=1)
                        else:
                            t_h = pl.step("vector",
                                (lambda pb, dst: lambda e: e.tensor_mul(
                                    dst, pb[:, :], dst))(pb, dst),
                                waits=[t_m, t_hact[(tile, nh)]], inc=1)
                        t_hact[(tile, nh)] = t_h
                        mk.set_pw(pb, t_h)
                        gw_war[sl] = t_h
                set_i += 1
                if DBG_PARANOID:
                    pl.barrier()
    mk.t_hact = t_hact
    t_gu_pe = pl.tok("tensor")
    if mk.stop_after == "p2hact":
        pl.dma("sync", mk.s_s[:, 0:44 * T // 2],
               hact[:].bitcast(F32)[:, 0:44 * T // 2], waits=[t_gu_pe, pl.tok("vector"), pl.tok("scalar")])
        return

    pl.barrier()
    y = mk.av(lay["f_y"])
    dnw = [mk.av(lay["f_dnw"][i]) for i in range(2)]
    dn_war = [t_gu_pe, t_gu_pe]
    t_y = {}
    t_sq_mm = None
    t_hact_all = pl.tok("vector")
    for zc in range(16):
        sl = zc % 2
        src = mk.dn_s[zc * DN_SLAB_ELEMS:(zc + 1) * DN_SLAB_ELEMS] \
            .rearrange("(p n) -> p n", p=128)
        t_w = pl.dma("sync", dnw[sl], src,
                     waits=[dn_war[sl]] + (mk.t_deq_gp if p == 0 and zc < 2 else []))
        for nh in range(2):
            pb = mk.p_mm[2 * sl + nh]
            t_m = None
            first = True
            for t_i in range(40):
                t_m = pl.step("tensor", (lambda sl, t_i, nh, pb, first: lambda e: e.matmul(
                    out=pb[:, :], lhsT=dnw[sl][:, 128 * t_i:128 * (t_i + 1)],
                    rhs=hact[:, t_i * T + nh * 512: t_i * T + (nh + 1) * 512],
                    start=first, stop=False,
                    skip_group_check=True))(sl, t_i, nh, pb, first),
                    waits=([t_w, t_hact_all, t_gu_pe] + mk.pw(pb)) if first else [])
                first = False
            for mp in range(4):
                t_i = 40 + mp
                last = (mp == 3)
                t_m = pl.step("tensor",
                    (lambda sl, t_i, nh, pb, last: lambda e: e.matmul(
                        out=pb[:, :],
                        lhsT=dnw[sl][:, 128 * t_i:128 * (t_i + 1)],
                        rhs=hact[:, t_i * T + nh * 512: t_i * T + (nh + 1) * 512],
                        start=False, stop=last,
                        skip_group_check=True))(sl, t_i, nh, pb, last),
                    inc=(1 if last else 0))
            dst = y[:, zc * T + nh * 512: zc * T + (nh + 1) * 512]
            t_ye = pl.step("vector", (lambda pb, dst: lambda e: e.tensor_copy(
                out=dst, in_=pb[:, :]))(pb, dst),
                waits=[t_m] + ([t_xnp] if zc < 2 else []), inc=1)
            t_y[(zc, nh)] = t_ye
            mk.set_pw(pb, t_ye)
            dn_war[sl] = t_ye
        if "ystats" in DBG_SKIP:
            continue
        t_sq = pl.step("scalar", (lambda zc: lambda e: e.activation(
            out=sq, in_=y[:, zc * T:(zc + 1) * T], func=AF.Square))(zc),
            waits=[t_y[(zc, 0)], t_y[(zc, 1)]] + ([t_sq_mm] if t_sq_mm else []),
            inc=1)
        for nh in range(2):
            bank = (mk.p_aw, mk.p_x)[nh]
            t_sq_mm = pl.step("tensor", (lambda zc, nh, bank: lambda e: e.matmul(
                out=bank[0:1, :], lhsT=mk.ones_col[:],
                rhs=sq[:, 512 * nh:512 * (nh + 1)],
                start=(zc == 0), stop=(zc == 15),
                skip_group_check=True))(zc, nh, bank),
                waits=[t_sq] + (mk.pw(bank) if zc == 0 else []),
                inc=(1 if nh == 1 else 0))
        if DBG_PARANOID:
            pl.barrier()
    t_mm2 = pl.tok("tensor")
    t_rb2 = rstd_rows(t_mm2, (mk.p_aw, mk.p_x), FFN_EPS) \
        if "ystats" not in DBG_SKIP else [pl.tok("vector")]
    if mk.stop_after == "p2down":
        pl.dma("gpsimd", mk.s_s[:, 0:16 * T], y,
               waits=[pl.tok("vector"), pl.tok("scalar"), pl.tok("tensor")])
        return

    pl.barrier()
    gbuf = mk.av(lay["f_gbuf"])
    srd = [mk.av(lay["f_srd"][i]) for i in range(2)]
    xrd = [mk.av(lay["f_xrd"][i]) for i in range(2)]
    ost = [mk.av(lay["f_ost"][i]) for i in range(2)]
    io_war = [None, None]
    t_out_last = None
    t_gb_prev = None
    for z in range(16):
        t_gm = pl.step("vector", (lambda z: lambda e: e.tensor_mul(
            gbuf, y[:, z * T:(z + 1) * T], rstd_b))(z),
            waits=t_rb2 + [t_y[(z, 0)], t_y[(z, 1)]] +
            ([t_gb_prev] if t_gb_prev else t_s_sy), inc=1)
        t_gs = pl.step("scalar", (lambda: lambda e: e.activation(
            out=gbuf, in_=gbuf, func=AF.Silu))(), waits=[t_gm], inc=1)
        for nh in range(2):
            sl = nh
            t_sr = pl.dma("sync", srd[sl],
                          mk.s_s[:, z * 4096 + t0 + 512 * nh:
                                 z * 4096 + t0 + 512 * (nh + 1)],
                          waits=[io_war[sl]] if io_war[sl] else t_s_sy)
            t_xr = pl.dma("sync", xrd[sl],
                          mk.x_in[128 * z:128 * (z + 1),
                                  t0 + 512 * nh:t0 + 512 * (nh + 1)])
            t_sg = pl.step("vector", (lambda sl, nh: lambda e: e.tensor_mul(
                srd[sl], srd[sl], gbuf[:, 512 * nh:512 * (nh + 1)]))(sl, nh),
                waits=[t_sr, t_gs], inc=1)
            t_o = pl.step("vector", (lambda sl: lambda e: e.tensor_add(
                ost[sl], xrd[sl], srd[sl]))(sl), waits=[t_xr], inc=1)
            t_od = pl.dma("sync",
                          mk.y_out[128 * z:128 * (z + 1),
                                   t0 + 512 * nh:t0 + 512 * (nh + 1)],
                          ost[sl], waits=[t_o])
            io_war[sl] = t_od
            t_out_last = t_od
            t_gb_prev = t_sg
        if DBG_PARANOID:
            pl.barrier()
    mk.pass_war = [t_out_last, pl.tok("vector"), pl.tok("scalar"),
                   pl.tok("tensor")] + pl.dma_toks("sync")


def emit_finish(mk):
    pl = mk.pl
    toks = [pl.tok(e) for e in Plan.ENGINES] + pl.dma_toks("sync") + \
        pl.dma_toks("gpsimd")
    pl.step("sync", lambda e: e.nop(), waits=toks)


def emit_all(mk):
    emit_constants(mk)
    emit_dequant(mk)
    if mk.stop_after == "dequant":
        emit_finish(mk)
        return
    emit_phase1(mk)
    if mk.stop_after in ("ln", "qproj", "kvproj", "phase1") or \
            (mk.stop_after or "").startswith("km"):
        emit_finish(mk)
        return
    emit_collective(mk)
    for p in range(mk.npass):
        emit_pass(mk, p)
        if mk.stop_after in ("p2blocks", "p2s", "p2hact", "p2down", "p2an"):
            break
        if mk.stop_after == "p2p1":
            break
    emit_finish(mk)


def build_nc(stop_after=None):
    nc = bass.Bass("TRN2", target_bir_lowering=False)
    es = ExitStack()
    with es:
        mk = MK(nc, es, stop_after)
        emit_all(mk)
        blk = es.enter_context(nc.Block())

        @blk.sync
        def _(e):
            mk.pl.replay("sync", e)

        @blk.gpsimd
        def _(e):
            mk.pl.replay("gpsimd", e)

        @blk.vector
        def _(e):
            mk.pl.replay("vector", e)

        @blk.scalar
        def _(e):
            mk.pl.replay("scalar", e)

        @blk.tensor
        def _(e):
            mk.pl.replay("tensor", e)
    return nc


def _col_slabs(wT):
    return np.ascontiguousarray(
        wT.reshape(16, 128, 16, 128).transpose(2, 1, 0, 3).reshape(16, 128, 2048))


def _prep_weights(qkv_w, out_w, gate_w, norm_w, gup, gus, dnp, dns):
    ph = PERM_H
    w_q = _col_slabs(qkv_w[_q_rows()][:, ph].T)
    kv = _kv_rows()
    w_k = _col_slabs(qkv_w[kv[:HID]][:, ph].T)
    w_kv = np.ascontiguousarray(qkv_w[kv][:, ph].T)
    w_gate = _col_slabs(gate_w[:, ph].T)
    w_outT = _col_slabs(out_w.T[:, ph])
    gcol = np.concatenate([PERM_I, IFF + PERM_I])
    gup_p = np.ascontiguousarray(gup.T[:, gcol])
    gus_p = np.ascontiguousarray(gus[(np.arange(JH) // 16)][:, gcol])
    dnp_p = np.ascontiguousarray(dnp.T[:, ph])
    dns_p = np.ascontiguousarray(dns[(np.arange(JI) // 16)][:, ph])
    return dict(w_q=w_q, w_k=w_k, w_kv=w_kv, w_gate=w_gate, w_out=w_outT,
                nw_in=norm_w, gup_in=gup_p, gus_in=gus_p, dnp_in=dnp_p,
                dns_in=dns_p)


def kernel(L0_attn_hidden_states, L0_attn_qkv_w, L0_attn_out_w, L0_attn_gate_w,
           L0_attn_norm_w, L0_ffn_gate_up_packed, L0_ffn_gate_up_scales,
           L0_ffn_down_packed, L0_ffn_down_scales):
    x = np.asarray(L0_attn_hidden_states)
    shared = _prep_weights(np.asarray(L0_attn_qkv_w), np.asarray(L0_attn_out_w),
                           np.asarray(L0_attn_gate_w), np.asarray(L0_attn_norm_w),
                           np.asarray(L0_ffn_gate_up_packed),
                           np.asarray(L0_ffn_gate_up_scales),
                           np.asarray(L0_ffn_down_packed),
                           np.asarray(L0_ffn_down_scales))
    in_maps = []
    for c in range(NCORES):
        b, half = c // 2, c % 2
        t0 = half * TCORE
        xs = np.ascontiguousarray(x[b, t0:t0 + TCORE, :][:, PERM_H].T)
        m = dict(shared)
        m["x_in"] = xs
        m["sgate_in"] = np.full((128, 1), float(half), np.float32)
        in_maps.append(m)
    nc = build_nc()
    res = run_bass_kernel_spmd(nc, in_maps, core_ids=list(range(NCORES)))
    out = np.empty((B, S, HID), np.float32)
    for c in range(NCORES):
        b, half = c // 2, c % 2
        t0 = half * TCORE
        out[b, t0:t0 + TCORE, PERM_H] = res.results[c]["y_out"]
    return out
